# version 01
# baseline (speedup 1.0000x reference)
import contextlib
import numpy as np

import concourse.bass as bass
import concourse.mybir as mybir
from concourse.bass_utils import run_bass_kernel_spmd
from concourse.alu_op_type import AluOpType as OP

F32 = mybir.dt.float32
BF16 = mybir.dt.bfloat16
I32 = mybir.dt.int32
U8 = mybir.dt.uint8
AF = mybir.ActivationFunctionType
AX = mybir.AxisListType

NCORES = 8
T = 2048
HALO = 128
TH = T + HALO
D = 1024
DI = 2048
NST = 16
DB_D = D // 128
DB_I = DI // 128
TC = 1024
NCH = T // TC
NTT = TH // 128

CC_TH = [(0, 512), (512, 512), (1024, 512), (1536, 512), (2048, 128)]
CC_T = [(0, 512), (512, 512), (1024, 512), (1536, 512)]

C_ONE, C_EPS, C_NHALF, C_HALF, C_INVD, C_NONE, C_I14, C_LN7, C_NLN14 = range(9)
CST_VALS = [1.0, 1e-5, -0.5, 0.5, 1.0 / 1024.0, -1.0, 1.0 / 14.0,
            float(np.log(7.0)), float(-np.log(14.0))]

ARENA_BYTES = 151552


class Sem:
    def __init__(self, es, nc, name):
        self.h = es.enter_context(nc.semaphore(name))
        self.v = 0


def build_nc(upto=99, dbg=()):
    nc = bass.Bass("TRN2", target_bir_lowering=False)
    dbg = set(dbg)

    x_h = nc.dram_tensor("x", [TH, D], F32, kind="ExternalInput").ap()
    wq_h = nc.dram_tensor("Wq", [D, D], F32, kind="ExternalInput").ap()
    fbias_h = nc.dram_tensor("fbias", [D], F32, kind="ExternalInput").ap()
    la_h = nc.dram_tensor("la", [D, NST], F32, kind="ExternalInput").ap()
    lb_h = nc.dram_tensor("lb", [NST, D], F32, kind="ExternalInput").ap()
    winT_h = nc.dram_tensor("WinT", [D, 2 * DI], F32, kind="ExternalInput").ap()
    wdtT_h = nc.dram_tensor("WdtT", [DI, DI], F32, kind="ExternalInput").ap()
    wbT_h = nc.dram_tensor("WBT", [DI, NST], F32, kind="ExternalInput").ap()
    wcT_h = nc.dram_tensor("WCT", [DI, NST], F32, kind="ExternalInput").ap()
    woutT_h = nc.dram_tensor("WoutT", [DI, D], F32, kind="ExternalInput").ap()
    wconv_h = nc.dram_tensor("Wconv", [DI, 4], F32, kind="ExternalInput").ap()
    bconv_h = nc.dram_tensor("bconv", [DI], F32, kind="ExternalInput").ap()
    bdt_h = nc.dram_tensor("bdt", [DI], F32, kind="ExternalInput").ap()
    alog_h = nc.dram_tensor("Alog", [DI, NST], F32, kind="ExternalInput").ap()
    ident_h = nc.dram_tensor("ident", [128, 128], F32, kind="ExternalInput").ap()
    ones_h = nc.dram_tensor("onesm", [128, 128], F32, kind="ExternalInput").ap()
    sel16_h = nc.dram_tensor("sel16", [NST, NST * 128], F32, kind="ExternalInput").ap()
    pmask_h = nc.dram_tensor("pmask", [128, 8], F32, kind="ExternalInput").ap()
    pmaski_h = nc.dram_tensor("pmaski", [128, 8], F32, kind="ExternalInput").ap()

    out_h = nc.dram_tensor("out", [T, D], F32, kind="ExternalOutput").ap()

    sp_xres = nc.dram_tensor("sp_xres", [128, DB_D * TH], F32, kind="Internal").ap()
    sp_xin = nc.dram_tensor("sp_xin", [128, DB_I * TH], BF16, kind="Internal").ap()
    sp_z = nc.dram_tensor("sp_z", [128, DB_I * T], BF16, kind="Internal").ap()
    sp_dt = nc.dram_tensor("sp_dt", [128, DB_I * T], F32, kind="Internal").ap()
    sp_dtu = nc.dram_tensor("sp_dtu", [128, DB_I * T], BF16, kind="Internal").ap()
    sp_g = nc.dram_tensor("sp_g", [128, DB_I * T], BF16, kind="Internal").ap()
    dgscr = nc.dram_tensor("dgscr", [64 * 128, 128], BF16, kind="Internal").ap()
    bounce = nc.dram_tensor("bounce", [128, 1], F32, kind="Internal").ap()
    cc_in = nc.dram_tensor("cc_in", [128, 512], F32, kind="Internal").ap()
    cc_out = nc.dram_tensor("cc_out", [8, 128, 512], F32, kind="Internal",
                            addr_space="Shared").ap()

    dbg_specs = {
        "d_xres": ([128, DB_D * TH], F32),
        "d_xln": ([128, DB_D * TH], BF16),
        "d_xin": ([128, DB_I * TH], BF16),
        "d_u": ([128, DB_I * T], BF16),
        "d_z": ([128, DB_I * T], BF16),
        "d_dt": ([128, DB_I * T], F32),
        "d_dtu": ([128, DB_I * T], BF16),
        "d_bm": ([NST, T], BF16),
        "d_cm": ([NST, T], BF16),
        "d_hp": ([128, 512], F32),
        "d_hinit": ([128, 256], F32),
        "d_g": ([128, DB_I * T], BF16),
        "d_wq": ([128, DB_D * D], BF16),
        "d_rows": ([1, 4 * TH], F32),
    }
    dbg_out = {}
    for name in dbg:
        shape, dt_ = dbg_specs[name]
        dbg_out[name] = nc.dram_tensor(name, shape, dt_, kind="ExternalOutput").ap()

    es = contextlib.ExitStack()
    with es:
        def sbuf(name, shape, dt_=F32):
            return es.enter_context(nc.sbuf_tensor(name, list(shape), dt_))

        def psum(name, shape, dt_=F32):
            return es.enter_context(nc.psum_tensor(name, list(shape), dt_))

        cst = sbuf("cst", (128, 16))
        ident_f = sbuf("ident_f", (128, 128))
        ident_b = sbuf("ident_b", (128, 128), BF16)
        ones_f = sbuf("ones_f", (128, 128))
        ones_b = sbuf("ones_b", (128, 128), BF16)
        sel16_b = sbuf("sel16_b", (NST, NST * 128), BF16)
        pmask = sbuf("pmask_s", (128, 8))
        pmaski = sbuf("pmaski_s", (128, 8))
        biasq = sbuf("biasq", (128, DB_D))
        qcol = sbuf("qcol", (128, 4))
        la_b = sbuf("la_b", (128, DB_D * NST), BF16)
        lb_b = sbuf("lb_b", (NST, D), BF16)
        t1_b = sbuf("t1_b", (NST, TH), BF16)
        bconv_c = sbuf("bconv_c", (128, DB_I))
        bdt_c = sbuf("bdt_c", (128, DB_I))
        a_c = sbuf("a_c", (128, DB_I * NST))
        sumdt = sbuf("sumdt", (128, DB_I))
        sumdt0 = sbuf("sumdt0", (128, DB_I))
        hst1 = sbuf("hst1", (128, 256))
        hst2 = sbuf("hst2", (128, 256))
        hinit = sbuf("hinit", (128, 256))
        hcA = sbuf("hcA", (128, 256))
        hcB = sbuf("hcB", (128, 256))
        hp = sbuf("hp", (128, 512))
        bmT_b = sbuf("bmT_b", (NST, T), BF16)
        cmT_b = sbuf("cmT_b", (NST, T), BF16)
        wq_b = sbuf("wq_b", (128, DB_D * D), BF16)
        wbc_b = sbuf("wbc_b", (128, 2 * DB_I * NST), BF16)
        wcv_b = sbuf("wcv_b", (128, 64), BF16)
        zrow_b = sbuf("zrow_b", (128, 128), BF16)
        sq_t = sbuf("sq_t", (128, 2 * 512), BF16)
        nrm_t = sbuf("nrm_t", (128, 512))
        ep_t = sbuf("ep_t", (128, 2 * 512))
        e1_t = sbuf("e1_t", (128, 2 * 512))
        dgt_t = sbuf("dgt_t", (128, 2 * 4 * 128), BF16)

        AR = sbuf("ARENA", (128, ARENA_BYTES), U8)

        def av(off_bytes, cols, dt_):
            sz = 4 if dt_ in (F32, I32) else (2 if dt_ == BF16 else 1)
            return AR[:, off_bytes:off_bytes + cols * sz].bitcast(dt_)

        x_res = av(0, DB_D * TH, F32)
        winb = av(0, DB_D * 4096, BF16)
        wdtb = av(0, DB_I * 2048, BF16)
        dt_t = av(0, 2 * TC, F32)
        dtu_t = av(8192, 2 * TC, BF16)
        zl_t = av(12288, 2 * TC, BF16)
        dA_t = av(16384, 2 * TC, F32)
        b_t = av(24576, TC, BF16)
        h_t = av(26624, 4 * TC, BF16)
        h1f = av(26624, 2 * TC, F32)
        yw_t = av(34816, 2 * TC, BF16)
        zs_t = av(38912, TC, F32)
        g_t = av(43008, 2 * TC, BF16)
        gat_t = av(47104, 8 * 512, F32)

        xln = av(69632, DB_D * TH, BF16)
        wob = av(69632, DB_I * 1024, BF16)
        u_v = av(69632, DB_I * T, BF16)
        reps = av(69632, NST * 2 * TC, BF16)
        ost_t = av(69632, 2 * 1024, F32)
        xt_st = [av(69632 + p * 4096, 1024, F32) for p in range(2)]
        qsl = [av(69632 + p * 4096, 1024, F32) for p in range(8)]
        rows = av(104448, TH, F32)[0:1, :]
        rows2 = av(113152, TH, F32)[0:1, :]
        rows3 = av(121856, TH, F32)[0:1, :]
        rows4 = av(130560, TH, F32)[0:1, :]
        sl5 = av(104448, 4096, F32)
        xin_st = [av(120832 + p * 4352, TH, BF16) for p in range(2)]
        zst = [av(129536 + p * 4096, T, BF16) for p in range(2)]
        xin6 = [av(135168 + p * 4352, TH, BF16) for p in range(2)]
        sl7 = av(135168, 2048, F32)
        dt7 = av(143360, T, F32)
        dtu7 = av(65536, T, BF16)
        sl9 = [av(102400 + p * 4096, 1024, F32) for p in range(2)]
        gst9 = [av(110592 + p * 1024, 512, BF16) for p in range(2)]

        PS_A = psum("PS_A", (128, 1024))
        PS_B = psum("PS_B", (128, 1024))
        PS_C = psum("PS_C", (128, 512))
        PS_D = psum("PS_D", (128, 512))
        PS_E = psum("PS_E", (128, 512))
        PS_F = psum("PS_F", (128, 512))
        BANKS8 = [PS_A[:, 0:512], PS_A[:, 512:1024], PS_B[:, 0:512],
                  PS_B[:, 512:1024], PS_C[:, :], PS_D[:, :], PS_E[:, :], PS_F[:, :]]

        sLD = Sem(es, nc, "sLD")
        sSP = Sem(es, nc, "sSP")
        sTE = Sem(es, nc, "sTE")
        sACT = Sem(es, nc, "sACT")
        sDVE = Sem(es, nc, "sDVE")
        sBAR = Sem(es, nc, "sBAR")
        sPA = Sem(es, nc, "sPA")
        sPB = Sem(es, nc, "sPB")
        sPC = Sem(es, nc, "sPC")
        sPD = Sem(es, nc, "sPD")
        sPE = Sem(es, nc, "sPE")

        block = es.enter_context(nc.Block())

        def barrier():
            tgt = sBAR.v + 5
            for sec in (block.sync, block.vector, block.scalar, block.tensor,
                        block.gpsimd):
                @sec
                def _(e):
                    e.drain()
                    e.sem_inc(sBAR.h, 1)
                    e.wait_ge(sBAR.h, tgt)
            sBAR.v = tgt

        @block.vector
        def _(v):
            for i, val in enumerate(CST_VALS):
                v.memset(cst[:, i:i + 1], val)
            v.memset(zrow_b[:], 0.0).then_inc(sDVE.h, 1)
        sDVE.v += 1

        @block.sync
        def _(sync):
            sync.dma_start(out=ident_f[:], in_=ident_h[:]).then_inc(sLD.h, 16)
            sync.dma_start(out=ones_f[:], in_=ones_h[:]).then_inc(sLD.h, 16)
            sync.dma_start(out=pmask[:], in_=pmask_h[:]).then_inc(sLD.h, 16)
            sync.dma_start(out=pmaski[:], in_=pmaski_h[:]).then_inc(sLD.h, 16)
            for g in range(DB_D):
                sync.dma_start(out=biasq[:, g:g + 1],
                               in_=fbias_h[g * 128:(g + 1) * 128].unsqueeze(1)).then_inc(sLD.h, 16)
                sync.dma_start(out=nrm_t[:, g * NST:(g + 1) * NST],
                               in_=la_h[g * 128:(g + 1) * 128, :]).then_inc(sLD.h, 16)
            for g in range(DB_I):
                sync.dma_start(out=bconv_c[:, g:g + 1],
                               in_=bconv_h[g * 128:(g + 1) * 128].unsqueeze(1)).then_inc(sLD.h, 16)
                sync.dma_start(out=bdt_c[:, g:g + 1],
                               in_=bdt_h[g * 128:(g + 1) * 128].unsqueeze(1)).then_inc(sLD.h, 16)
                sync.dma_start(out=e1_t[:, g * NST:(g + 1) * NST],
                               in_=alog_h[g * 128:(g + 1) * 128, :]).then_inc(sLD.h, 16)
                sync.dma_start(out=nrm_t[:, 256 + g * 4:256 + (g + 1) * 4],
                               in_=wconv_h[g * 128:(g + 1) * 128, :]).then_inc(sLD.h, 16)
            sync.dma_start(out=ep_t[:NST, 0:1024], in_=lb_h[:]).then_inc(sLD.h, 16)
            sync.dma_start(out=gat_t[:NST, 0:NST * 128],
                           in_=sel16_h[:]).then_inc(sLD.h, 16)
        sLD.v += (4 + 2 * DB_D + 4 * DB_I + 2) * 16
        p0c = sLD.v

        @block.scalar
        def _(s):
            s.wait_ge(sLD.h, p0c)
            s.wait_ge(sDVE.h, 1)
            s.copy(out=ident_b[:], in_=ident_f[:])
            s.copy(out=ones_b[:], in_=ones_f[:])
            for kb in range(DB_D):
                s.copy(out=la_b[:, kb * NST:(kb + 1) * NST],
                       in_=nrm_t[:, kb * NST:(kb + 1) * NST])
            s.copy(out=lb_b[:], in_=ep_t[:NST, 0:1024])
            s.copy(out=sel16_b[:], in_=gat_t[:NST, 0:NST * 128])
            s.copy(out=wcv_b[:], in_=nrm_t[:, 256:256 + 64])
            s.activation(out=e1_t[:, 512:512 + DB_I * NST],
                         in_=e1_t[:, 0:DB_I * NST], func=AF.Exp)
            s.activation(out=a_c[:], in_=e1_t[:, 512:512 + DB_I * NST],
                         func=AF.Copy, scale=cst[:, C_NONE:C_NONE + 1]
                         ).then_inc(sACT.h, 1)
        sACT.v += 1

        barrier()

        x_ld_base = sLD.v
        te0_base = sTE.v
        act0_base = sACT.v

        @block.sync
        def _(sync):
            for g in range(NTT):
                par = g % 2
                if g >= 2:
                    sync.wait_ge(sTE.h, te0_base + 2 * (g - 1))
                sync.dma_start(out=xt_st[par][:],
                               in_=x_h[g * 128:(g + 1) * 128, :]).then_inc(sLD.h, 16)
        sLD.v += NTT * 16

        @block.tensor
        def _(t):
            for g in range(NTT):
                par = g % 2
                t.wait_ge(sLD.h, x_ld_base + (g + 1) * 16)
                for half in range(2):
                    if g * 2 + half >= 2:
                        t.wait_ge(sACT.h, act0_base + g * 2 + half - 1)
                    pt = PS_A if half == 0 else PS_B
                    for q in range(4):
                        db = half * 4 + q
                        ins = t.transpose(pt[:, q * 128:(q + 1) * 128],
                                          xt_st[par][:, db * 128:(db + 1) * 128],
                                          ident_f[:])
                    ins.then_inc(sTE.h, 1)
        sTE.v += NTT * 2

        @block.scalar
        def _(s):
            for g in range(NTT):
                for half in range(2):
                    s.wait_ge(sTE.h, te0_base + g * 2 + half + 1)
                    pt = PS_A if half == 0 else PS_B
                    for q in range(4):
                        db = half * 4 + q
                        ins = s.copy(out=x_res[:, db * TH + g * 128: db * TH + (g + 1) * 128],
                                     in_=pt[:, q * 128:(q + 1) * 128])
                    ins.then_inc(sACT.h, 1)
        sACT.v += NTT * 2

        barrier()

        q_ld_base = sLD.v
        dve1_base = sDVE.v

        @block.sync
        def _(sync):
            for kb in range(DB_D):
                sync.dma_start(out=qsl[kb][:],
                               in_=wq_h[kb * 128:(kb + 1) * 128, :]).then_inc(sLD.h, 16)
        sLD.v += DB_D * 16

        @block.vector
        def _(v):
            for kb in range(DB_D):
                v.wait_ge(sLD.h, q_ld_base + (kb + 1) * 16)
                dst = qcol[:, 3:4] if kb == 0 else nrm_t[:, 0:1]
                ins = v.tensor_reduce(out=dst, in_=qsl[kb][:], axis=AX.X, op=OP.max,
                                      apply_absolute_value=True)
                if kb > 0:
                    ins = v.tensor_tensor(out=qcol[:, 3:4], in0=qcol[:, 3:4],
                                          in1=nrm_t[:, 0:1], op=OP.max)
                ins.then_inc(sDVE.h, 1)
        sDVE.v += DB_D
        dve_q1 = sDVE.v

        @block.gpsimd
        def _(g):
            g.wait_ge(sDVE.h, dve_q1)
            g.dma_start(out=bounce[:, 0:1], in_=qcol[:, 3:4]).then_inc(sSP.h, 16)
            g.wait_ge(sSP.h, sSP.v + 16)
            g.dma_start(out=rows[0:1, 0:128],
                        in_=bounce.rearrange("p q -> q p")[0:1, 0:128]
                        ).then_inc(sSP.h, 16)
        sSP.v += 32
        sp_q1 = sSP.v

        @block.vector
        def _(v):
            v.wait_ge(sSP.h, sp_q1)
            v.tensor_reduce(out=rows[0:1, 128:129], in_=rows[0:1, 0:128],
                            axis=AX.X, op=OP.max)
            v.drain()
            v.sem_inc(sDVE.h, 1)
        sDVE.v += 1
        dve_q2 = sDVE.v

        @block.scalar
        def _(s):
            s.wait_ge(sDVE.h, dve_q2)
            s.activation(out=rows[0:1, 129:130], in_=rows[0:1, 128:129], func=AF.Ln)
            s.activation(out=rows[0:1, 130:131], in_=rows[0:1, 129:130],
                         func=AF.Exp, scale=cst[0:1, C_NONE:C_NONE + 1],
                         bias=cst[0:1, C_LN7:C_LN7 + 1])
            s.activation(out=rows[0:1, 131:132], in_=rows[0:1, 129:130],
                         func=AF.Exp, scale=1.0,
                         bias=cst[0:1, C_NLN14:C_NLN14 + 1])
            s.drain()
            s.sem_inc(sACT.h, 1)
        sACT.v += 1
        act_ln = sACT.v

        @block.tensor
        def _(t):
            t.wait_ge(sACT.h, act_ln)
            t.matmul(PS_C[:, 0:2], ones_f[0:1, :], rows[0:1, 130:132],
                     start=True, stop=True)
            t.drain()
            t.sem_inc(sTE.h, 1)
        sTE.v += 1
        te_q1 = sTE.v

        @block.scalar
        def _(s):
            s.wait_ge(sTE.h, te_q1)
            s.copy(out=qcol[:, 1:3], in_=PS_C[:, 0:2])
            s.drain()
            s.sem_inc(sACT.h, 1)
        sACT.v += 1
        act_q1 = sACT.v

        @block.vector
        def _(v):
            v.wait_ge(sACT.h, act_q1)
            v.tensor_scalar(out=biasq[:], in0=biasq[:],
                            scalar1=cst[:, C_HALF:C_HALF + 1], scalar2=None, op0=OP.mult)
            for kb in range(DB_D):
                for hh in range(2):
                    v.tensor_scalar(out=nrm_t[:, 0:512],
                                    in0=qsl[kb][:, hh * 512:(hh + 1) * 512],
                                    scalar1=qcol[:, 1:2], scalar2=None, op0=OP.mult)
                    v.drain()
                    v.tensor_copy(out=ep_t[:, 0:512].bitcast(I32), in_=nrm_t[:, 0:512])
                    v.drain()
                    ins = v.tensor_scalar(
                        out=wq_b[:, kb * 1024 + hh * 512: kb * 1024 + hh * 512 + 512],
                        in0=ep_t[:, 0:512].bitcast(I32),
                        scalar1=7.0, scalar2=-8.0, op0=OP.min, op1=OP.max)
                ins.then_inc(sDVE.h, 1)
        sDVE.v += DB_D

        barrier()
        if upto <= 1:
            return _dump(nc, block, dbg, dbg_out, locals())

        def emit_ln(width, src_off):
            ccs = CC_TH if width == TH else CC_T
            nchunk = len(ccs)
            lnpx, lnpq, lnpr, lnpr2 = PS_E, PS_F, PS_C, PS_D

            @block.vector
            def _(v):
                for db in range(DB_D):
                    ins = v.tensor_copy(
                        out=xln[:, db * width:(db + 1) * width],
                        in_=x_res[:, db * TH + src_off: db * TH + src_off + width])
                ins.then_inc(sDVE.h, 1)
            sDVE.v += 1
            cast_done = sDVE.v
            act_base = sACT.v
            te_base = sTE.v

            @block.scalar
            def _(s):
                s.wait_ge(sDVE.h, cast_done)
                for ci in range(nchunk):
                    c0, cw = ccs[ci]
                    for db in range(DB_D):
                        k = ci * DB_D + db
                        par = k % 2
                        if k >= 2:
                            s.wait_ge(sTE.h, te_base + k - 1)
                        s.activation(out=sq_t[:, par * 512:par * 512 + cw],
                                     in_=xln[:, db * width + c0: db * width + c0 + cw],
                                     func=AF.Square).then_inc(sACT.h, 1)
                    s.wait_ge(sTE.h, te_base + (ci + 1) * DB_D)
                    s.copy(out=rows[0:1, c0:c0 + cw], in_=lnpx[0:1, 0:cw])
                    s.copy(out=rows2[0:1, c0:c0 + cw],
                           in_=lnpq[0:1, 0:cw]).then_inc(sACT.h, 1)
            sACT.v += nchunk * (DB_D + 1)

            @block.tensor
            def _(t):
                for ci in range(nchunk):
                    c0, cw = ccs[ci]
                    for db in range(DB_D):
                        k = ci * DB_D + db
                        par = k % 2
                        t.wait_ge(sACT.h, act_base + ci * (DB_D + 1) + db + 1)
                        t.matmul(lnpx[0:1, 0:cw], ones_b[:, 0:1],
                                 xln[:, db * width + c0: db * width + c0 + cw],
                                 start=(db == 0), stop=(db == DB_D - 1))
                        t.matmul(lnpq[0:1, 0:cw], ones_b[:, 0:1],
                                 sq_t[:, par * 512:par * 512 + cw],
                                 start=(db == 0), stop=(db == DB_D - 1)
                                 ).then_inc(sTE.h, 1)
            sTE.v += nchunk * DB_D
            stats_done = sACT.v

            @block.vector
            def _(v):
                v.wait_ge(sACT.h, stats_done)
                v.tensor_scalar(out=rows[0:1, 0:width], in0=rows[0:1, 0:width],
                                scalar1=cst[0:1, C_INVD:C_INVD + 1], scalar2=None,
                                op0=OP.mult)
                v.tensor_scalar(out=rows2[0:1, 0:width], in0=rows2[0:1, 0:width],
                                scalar1=cst[0:1, C_INVD:C_INVD + 1], scalar2=None,
                                op0=OP.mult)
                v.tensor_tensor(out=rows3[0:1, 0:width], in0=rows[0:1, 0:width],
                                in1=rows[0:1, 0:width], op=OP.mult)
                v.tensor_tensor(out=rows3[0:1, 0:width], in0=rows2[0:1, 0:width],
                                in1=rows3[0:1, 0:width], op=OP.subtract
                                ).then_inc(sDVE.h, 1)
            sDVE.v += 1
            row_done = sDVE.v

            @block.scalar
            def _(s):
                s.wait_ge(sDVE.h, row_done)
                s.activation(out=rows4[0:1, 0:width], in_=rows3[0:1, 0:width],
                             func=AF.Ln, bias=cst[0:1, C_EPS:C_EPS + 1], scale=1.0)
                s.activation(out=rows3[0:1, 0:width], in_=rows4[0:1, 0:width],
                             func=AF.Exp, scale=cst[0:1, C_NHALF:C_NHALF + 1]
                             ).then_inc(sACT.h, 1)
            sACT.v += 1
            rstd_done = sACT.v

            @block.vector
            def _(v):
                v.wait_ge(sACT.h, rstd_done)
                v.tensor_tensor(out=rows4[0:1, 0:width], in0=rows[0:1, 0:width],
                                in1=rows3[0:1, 0:width], op=OP.mult
                                ).then_inc(sDVE.h, 1)
            sDVE.v += 1
            mr_done = sDVE.v
            te_rep_base = sTE.v
            dve_nrm_base = sDVE.v

            @block.tensor
            def _(t):
                t.wait_ge(sDVE.h, mr_done)
                for ci in range(nchunk):
                    c0, cw = ccs[ci]
                    if ci >= 1:
                        t.wait_ge(sDVE.h, dve_nrm_base + ci * DB_D)
                    t.matmul(lnpr[:, 0:cw], ones_f[0:1, :], rows3[0:1, c0:c0 + cw],
                             start=True, stop=True)
                    t.matmul(lnpr2[:, 0:cw], ones_f[0:1, :], rows4[0:1, c0:c0 + cw],
                             start=True, stop=True).then_inc(sTE.h, 1)
            sTE.v += nchunk

            @block.vector
            def _(v):
                for ci in range(nchunk):
                    c0, cw = ccs[ci]
                    v.wait_ge(sTE.h, te_rep_base + ci + 1)
                    for db in range(DB_D):
                        v.tensor_tensor(out=nrm_t[:, 0:cw],
                                        in0=x_res[:, db * TH + src_off + c0:
                                                  db * TH + src_off + c0 + cw],
                                        in1=lnpr[:, 0:cw], op=OP.mult)
                        v.tensor_tensor(out=xln[:, db * width + c0: db * width + c0 + cw],
                                        in0=nrm_t[:, 0:cw],
                                        in1=lnpr2[:, 0:cw], op=OP.subtract
                                        ).then_inc(sDVE.h, 1)
            sDVE.v += nchunk * DB_D

            barrier()

        def emit_ffn(width, dst_off):
            ccs = CC_TH if width == TH else CC_T
            nchunk = len(ccs)
            fpl, fpa, fpb = PS_E, PS_C, PS_D
            te_l_base = sTE.v
            act_l_base = sACT.v

            @block.tensor
            def _(t):
                t.wait_ge(sDVE.h, sDVE.v)
                for ci in range(nchunk):
                    c0, cw = ccs[ci]
                    if ci >= 1:
                        t.wait_ge(sACT.h, act_l_base + ci)
                    for kb in range(DB_D):
                        ins = t.matmul(fpl[:NST, 0:cw],
                                       la_b[:, kb * NST:(kb + 1) * NST],
                                       xln[:, kb * width + c0: kb * width + c0 + cw],
                                       start=(kb == 0), stop=(kb == DB_D - 1))
                    ins.then_inc(sTE.h, 1)
            sTE.v += nchunk

            @block.scalar
            def _(s):
                for ci in range(nchunk):
                    c0, cw = ccs[ci]
                    s.wait_ge(sTE.h, te_l_base + ci + 1)
                    s.copy(out=t1_b[:NST, c0:c0 + cw],
                           in_=fpl[:NST, 0:cw]).then_inc(sACT.h, 1)
            sACT.v += nchunk
            t1_done = sACT.v

            order = [(fo, ci) for fo in range(DB_D) for ci in range(nchunk)]
            nit = len(order)
            te_m_base = sTE.v
            dve_m_base = sDVE.v

            @block.tensor
            def _(t):
                t.wait_ge(sACT.h, t1_done)
                for k, (fo, ci) in enumerate(order):
                    c0, cw = ccs[ci]
                    if k >= 1:
                        t.wait_ge(sACT.h, t1_done + k)
                        t.wait_ge(sDVE.h, dve_m_base + k)
                    for kb in range(DB_D):
                        t.matmul(fpa[:, 0:cw],
                                 wq_b[:, kb * 1024 + fo * 128: kb * 1024 + (fo + 1) * 128],
                                 xln[:, kb * width + c0: kb * width + c0 + cw],
                                 start=(kb == 0), stop=(kb == DB_D - 1))
                    t.matmul(fpb[:, 0:cw],
                             lb_b[:, fo * 128:(fo + 1) * 128],
                             t1_b[:NST, c0:c0 + cw],
                             start=True, stop=True).then_inc(sTE.h, 1)
            sTE.v += nit

            @block.scalar
            def _(s):
                for k, (fo, ci) in enumerate(order):
                    c0, cw = ccs[ci]
                    s.wait_ge(sTE.h, te_m_base + k + 1)
                    if k >= 2:
                        s.wait_ge(sDVE.h, dve_m_base + k - 1)
                    par = k % 2
                    s.activation(out=ep_t[:, par * 512:par * 512 + cw],
                                 in_=fpa[:, 0:cw], func=AF.Identity,
                                 scale=qcol[:, 2:3],
                                 bias=biasq[:, fo:fo + 1]).then_inc(sACT.h, 1)
            sACT.v += nit

            @block.vector
            def _(v):
                for k, (fo, ci) in enumerate(order):
                    c0, cw = ccs[ci]
                    v.wait_ge(sACT.h, t1_done + k + 1)
                    par = k % 2
                    dcol = fo * TH + dst_off + c0
                    v.tensor_tensor(out=x_res[:, dcol:dcol + cw],
                                    in0=x_res[:, dcol:dcol + cw],
                                    in1=ep_t[:, par * 512:par * 512 + cw], op=OP.add)
                    v.scalar_tensor_tensor(out=x_res[:, dcol:dcol + cw],
                                           in0=fpb[:, 0:cw],
                                           scalar=cst[:, C_HALF:C_HALF + 1],
                                           in1=x_res[:, dcol:dcol + cw],
                                           op0=OP.mult, op1=OP.add).then_inc(sDVE.h, 1)
            sDVE.v += nit

            barrier()

        emit_ln(TH, 0)
        if upto <= 2:
            return _dump(nc, block, dbg, dbg_out, locals())
        emit_ffn(TH, 0)
        if upto <= 3:
            return _dump(nc, block, dbg, dbg_out, locals())
        emit_ln(TH, 0)
        if upto <= 4:
            return _dump(nc, block, dbg, dbg_out, locals())

        @block.gpsimd
        def _(g):
            g.wait_ge(sBAR.h, sBAR.v)
            g.dma_start(out=sp_xres[:], in_=x_res[:]).then_inc(sSP.h, 16)
        sSP.v += 16
        barrier()

        w5_base = sLD.v
        cast5_base = sACT.v

        @block.sync
        def _(sync):
            for kb in range(DB_D):
                if kb >= 1:
                    sync.wait_ge(sACT.h, cast5_base + kb)
                sync.dma_start(out=sl5[:],
                               in_=winT_h[kb * 128:(kb + 1) * 128, :]).then_inc(sLD.h, 16)
        sLD.v += DB_D * 16

        @block.scalar
        def _(s):
            for kb in range(DB_D):
                s.wait_ge(sLD.h, w5_base + (kb + 1) * 16)
                s.copy(out=winb[:, kb * 4096:(kb + 1) * 4096],
                       in_=sl5[:]).then_inc(sACT.h, 1)
        sACT.v += DB_D
        cast5_done = sACT.v

        order5 = [(xo, ci) for xo in range(32) for ci in range(len(CC_TH))]
        te5_base = sTE.v
        act5_base = sACT.v
        sp5_base = sSP.v

        @block.tensor
        def _(t):
            t.wait_ge(sACT.h, cast5_done)
            for k, (xo, ci) in enumerate(order5):
                c0, cw = CC_TH[ci]
                par = k % 2
                pt = PS_A if par == 0 else PS_B
                if k >= 2:
                    t.wait_ge(sACT.h, cast5_done + k - 1)
                for kb in range(DB_D):
                    ins = t.matmul(pt[:, 0:cw],
                                   winb[:, kb * 4096 + xo * 128: kb * 4096 + (xo + 1) * 128],
                                   xln[:, kb * TH + c0: kb * TH + c0 + cw],
                                   start=(kb == 0), stop=(kb == DB_D - 1))
                ins.then_inc(sTE.h, 1)
        sTE.v += len(order5)

        @block.scalar
        def _(s):
            for k, (xo, ci) in enumerate(order5):
                c0, cw = CC_TH[ci]
                par = k % 2
                pt = PS_A if par == 0 else PS_B
                xp = xo % 2
                s.wait_ge(sTE.h, te5_base + k + 1)
                if ci == 0 and xo >= 2:
                    s.wait_ge(sSP.h, sp5_base + (xo - 1) * 16)
                if xo < 16:
                    s.copy(out=xin_st[xp][:, c0:c0 + cw],
                           in_=pt[:, 0:cw]).then_inc(sACT.h, 1)
                elif c0 == 0:
                    s.copy(out=zst[xp][:, 0:cw - HALO],
                           in_=pt[:, HALO:cw]).then_inc(sACT.h, 1)
                else:
                    s.copy(out=zst[xp][:, c0 - HALO:c0 - HALO + cw],
                           in_=pt[:, 0:cw]).then_inc(sACT.h, 1)
        sACT.v += len(order5)

        @block.gpsimd
        def _(g):
            for xo in range(32):
                xp = xo % 2
                g.wait_ge(sACT.h, act5_base + (xo + 1) * len(CC_TH))
                if xo < 16:
                    g.dma_start(out=sp_xin[:, xo * TH:(xo + 1) * TH],
                                in_=xin_st[xp][:]).then_inc(sSP.h, 16)
                else:
                    g.dma_start(out=sp_z[:, (xo - 16) * T:(xo - 15) * T],
                                in_=zst[xp][:]).then_inc(sSP.h, 16)
        sSP.v += 32 * 16

        barrier()
        if upto <= 5:
            return _dump(nc, block, dbg, dbg_out, locals())

        @block.gpsimd
        def _(g):
            g.wait_ge(sBAR.h, sBAR.v)
            for tt in range(64):
                g.dma_start(out=dgscr[tt * 128:(tt + 1) * 128, :],
                            in_=zrow_b[:]).then_inc(sSP.h, 16)
            g.wait_ge(sSP.h, sSP.v + 64 * 16)
            flat = dgscr.rearrange("a b -> (a b)")
            for db in range(DB_I):
                for o in range(4):
                    tt = db * 4 + o
                    dv = flat[tt * 16384: tt * 16384 + 127 * 129 + 1: 129].unsqueeze(1)
                    g.dma_start(out=dv, in_=wcv_b[:, db * 4 + o: db * 4 + o + 1],
                                allow_slow_non_contiguous=True).then_inc(sSP.h, 16)
        sSP.v += 128 * 16
        dg_done = sSP.v

        ld6_base = sLD.v
        te6_base = sTE.v
        act6_base = sACT.v

        @block.sync
        def _(sync):
            sync.wait_ge(sSP.h, dg_done)
            for db in range(DB_I):
                par = db % 2
                if db >= 2:
                    sync.wait_ge(sTE.h, te6_base + (db - 1) * 4)
                for o in range(4):
                    tt = db * 4 + o
                    sync.dma_start(out=dgt_t[:, par * 512 + o * 128:par * 512 + (o + 1) * 128],
                                   in_=dgscr[tt * 128:(tt + 1) * 128, :]
                                   ).then_inc(sLD.h, 16)
                sync.dma_start(out=xin6[par][:],
                               in_=sp_xin[:, db * TH:(db + 1) * TH]).then_inc(sLD.h, 16)
        sLD.v += DB_I * 80

        @block.tensor
        def _(t):
            for db in range(DB_I):
                par = db % 2
                t.wait_ge(sLD.h, ld6_base + (db + 1) * 80)
                for ci, (c0, cw) in enumerate(CC_T):
                    k = db * 4 + ci
                    pp = k % 2
                    pt = PS_A if pp == 0 else PS_B
                    if k >= 2:
                        t.wait_ge(sACT.h, act6_base + k - 1)
                    for o in range(4):
                        src0 = HALO + c0 - 3 + o
                        ins = t.matmul(pt[:, 0:cw],
                                       dgt_t[:, par * 512 + o * 128: par * 512 + (o + 1) * 128],
                                       xin6[par][:, src0:src0 + cw],
                                       start=(o == 0), stop=(o == 3))
                    ins.then_inc(sTE.h, 1)
        sTE.v += DB_I * 4

        @block.scalar
        def _(s):
            for db in range(DB_I):
                for ci, (c0, cw) in enumerate(CC_T):
                    k = db * 4 + ci
                    pp = k % 2
                    pt = PS_A if pp == 0 else PS_B
                    s.wait_ge(sTE.h, te6_base + k + 1)
                    s.activation(out=u_v[:, db * T + c0: db * T + c0 + cw],
                                 in_=pt[:, 0:cw], func=AF.Silu,
                                 bias=bconv_c[:, db:db + 1], scale=1.0
                                 ).then_inc(sACT.h, 1)
        sACT.v += DB_I * 4

        barrier()
        if upto <= 6:
            return _dump(nc, block, dbg, dbg_out, locals())

        w7_base = sLD.v
        cast7_base = sACT.v

        @block.sync
        def _(sync):
            for kb in range(DB_I):
                if kb >= 1:
                    sync.wait_ge(sACT.h, cast7_base + kb)
                sync.dma_start(out=sl7[:],
                               in_=wdtT_h[kb * 128:(kb + 1) * 128, :]).then_inc(sLD.h, 16)
            for g in range(DB_I):
                sync.dma_start(out=nrm_t[:, g * NST:(g + 1) * NST],
                               in_=wbT_h[g * 128:(g + 1) * 128, :]).then_inc(sLD.h, 16)
                sync.dma_start(out=ep_t[:, g * NST:(g + 1) * NST],
                               in_=wcT_h[g * 128:(g + 1) * 128, :]).then_inc(sLD.h, 16)
        sLD.v += (DB_I + 2 * DB_I) * 16

        @block.scalar
        def _(s):
            for kb in range(DB_I):
                s.wait_ge(sLD.h, w7_base + (kb + 1) * 16)
                s.copy(out=wdtb[:, kb * 2048:(kb + 1) * 2048],
                       in_=sl7[:]).then_inc(sACT.h, 1)
            s.wait_ge(sLD.h, w7_base + (DB_I + 2 * DB_I) * 16)
            s.copy(out=wbc_b[:, 0:DB_I * NST], in_=nrm_t[:, 0:256])
            s.copy(out=wbc_b[:, DB_I * NST:2 * DB_I * NST],
                   in_=ep_t[:, 0:256]).then_inc(sACT.h, 1)
        sACT.v += DB_I + 1
        cast7_done = sACT.v

        te7_base = sTE.v
        act7_base = sACT.v
        dve7_base = sDVE.v
        sp7_base = sSP.v

        @block.tensor
        def _(t):
            t.wait_ge(sACT.h, cast7_done)
            for xo in range(DB_I):
                for ci, (c0, cw) in enumerate(CC_T):
                    k = xo * 4 + ci
                    pp = k % 2
                    pt = PS_A if pp == 0 else PS_B
                    if k >= 2:
                        t.wait_ge(sACT.h, act7_base + 2 * (k - 1))
                    for kb in range(DB_I):
                        ins = t.matmul(pt[:, 0:cw],
                                       wdtb[:, kb * 2048 + xo * 128: kb * 2048 + (xo + 1) * 128],
                                       u_v[:, kb * T + c0: kb * T + c0 + cw],
                                       start=(kb == 0), stop=(kb == DB_I - 1))
                    ins.then_inc(sTE.h, 1)
        sTE.v += DB_I * 4

        @block.scalar
        def _(s):
            for xo in range(DB_I):
                if xo >= 1:
                    s.wait_ge(sSP.h, sp7_base + xo * 32)
                for ci, (c0, cw) in enumerate(CC_T):
                    k = xo * 4 + ci
                    pp = k % 2
                    pt = PS_A if pp == 0 else PS_B
                    s.wait_ge(sTE.h, te7_base + k + 1)
                    s.activation(out=e1_t[:, pp * 512:pp * 512 + cw],
                                 in_=pt[:, 0:cw], func=AF.Exp,
                                 bias=bdt_c[:, xo:xo + 1], scale=1.0).then_inc(sACT.h, 1)
                    s.activation(out=dt7[:, c0:c0 + cw],
                                 in_=e1_t[:, pp * 512:pp * 512 + cw], func=AF.Ln,
                                 bias=cst[:, C_ONE:C_ONE + 1], scale=1.0
                                 ).then_inc(sACT.h, 1)
        sACT.v += DB_I * 8

        @block.vector
        def _(v):
            for xo in range(DB_I):
                v.wait_ge(sACT.h, act7_base + (xo + 1) * 8)
                if xo >= 1:
                    v.wait_ge(sSP.h, sp7_base + xo * 32)
                v.tensor_reduce(out=sumdt[:, xo:xo + 1], in_=dt7[:],
                                axis=AX.X, op=OP.add)
                v.tensor_reduce(out=sumdt0[:, xo:xo + 1], in_=dt7[:, 0:TC],
                                axis=AX.X, op=OP.add)
                v.tensor_tensor(out=dtu7[:], in0=dt7[:],
                                in1=u_v[:, xo * T:(xo + 1) * T], op=OP.mult
                                ).then_inc(sDVE.h, 1)
        sDVE.v += DB_I

        @block.gpsimd
        def _(g):
            for xo in range(DB_I):
                g.wait_ge(sDVE.h, dve7_base + xo + 1)
                g.dma_start(out=sp_dt[:, xo * T:(xo + 1) * T],
                            in_=dt7[:]).then_inc(sSP.h, 16)
                g.dma_start(out=sp_dtu[:, xo * T:(xo + 1) * T],
                            in_=dtu7[:]).then_inc(sSP.h, 16)
        sSP.v += DB_I * 32

        te7b_base = sTE.v
        act7b_base = sACT.v

        @block.tensor
        def _(t):
            for w2 in range(2):
                for ci, (c0, cw) in enumerate(CC_T):
                    k = w2 * 4 + ci
                    if k >= 1:
                        t.wait_ge(sACT.h, act7b_base + k)
                    for kb in range(DB_I):
                        ins = t.matmul(PS_E[:NST, 0:cw],
                                       wbc_b[:, w2 * DB_I * NST + kb * NST:
                                             w2 * DB_I * NST + (kb + 1) * NST],
                                       u_v[:, kb * T + c0: kb * T + c0 + cw],
                                       start=(kb == 0), stop=(kb == DB_I - 1))
                    ins.then_inc(sTE.h, 1)
        sTE.v += 8

        @block.scalar
        def _(s):
            for w2 in range(2):
                dstf = bmT_b if w2 == 0 else cmT_b
                for ci, (c0, cw) in enumerate(CC_T):
                    k = w2 * 4 + ci
                    s.wait_ge(sTE.h, te7b_base + k + 1)
                    s.copy(out=dstf[:NST, c0:c0 + cw],
                           in_=PS_E[:NST, 0:cw]).then_inc(sACT.h, 1)
        sACT.v += 8

        barrier()
        if upto <= 7:
            return _dump(nc, block, dbg, dbg_out, locals())

        @block.vector
        def _(v):
            v.wait_ge(sBAR.h, sBAR.v)
            for db in range(DB_I):
                v.tensor_scalar(out=e1_t[:, db * NST:(db + 1) * NST],
                                in0=a_c[:, db * NST:(db + 1) * NST],
                                scalar1=sumdt[:, db:db + 1], scalar2=None,
                                op0=OP.mult)
            v.drain()
            v.sem_inc(sDVE.h, 1)
        sDVE.v += 1
        parg_done = sDVE.v

        @block.scalar
        def _(s):
            s.wait_ge(sDVE.h, parg_done)
            s.activation(out=hp[:, 256:512], in_=e1_t[:, 0:DB_I * NST], func=AF.Exp
                         ).then_inc(sACT.h, 1)
        sACT.v += 1

        barrier()

        def emit_reps(ch, with_c):
            items = []
            for n in range(NST):
                for w2 in range(2 if with_c else 1):
                    for c2 in range(2):
                        items.append((n, w2, c2))
            te_base = sTE.v
            dve_base = sDVE.v

            @block.tensor
            def _(t):
                for j, (n, w2, c2) in enumerate(items):
                    pp = j % 2
                    pt = PS_A if pp == 0 else PS_B
                    if j >= 2:
                        t.wait_ge(sDVE.h, dve_base + j - 1)
                    srcT = bmT_b if w2 == 0 else cmT_b
                    t.matmul(pt[:, 0:512],
                             sel16_b[:NST, n * 128:(n + 1) * 128],
                             srcT[:NST, ch * TC + c2 * 512: ch * TC + (c2 + 1) * 512],
                             start=True, stop=True).then_inc(sTE.h, 1)
            sTE.v += len(items)

            @block.vector
            def _(v):
                for j, (n, w2, c2) in enumerate(items):
                    pp = j % 2
                    pt = PS_A if pp == 0 else PS_B
                    v.wait_ge(sTE.h, te_base + j + 1)
                    dst0 = n * 2048 + w2 * TC + c2 * 512
                    v.tensor_copy(out=reps[:, dst0:dst0 + 512],
                                  in_=pt[:, 0:512]).then_inc(sDVE.h, 1)
            sDVE.v += len(items)
            barrier()

        def emit_scan_pass(ch, pass2):
            nload = 3 if pass2 else 2
            PAPER = 17 if pass2 else 16
            ld_base = sLD.v
            pa_base = sPA.v
            pb_base = sPB.v
            pc_base = sPC.v
            pd_base = sPD.v
            sp_base = sSP.v
            te_x_base = sTE.v

            @block.sync
            def _(sync):
                for db in range(DB_I):
                    par2 = db % 2
                    if db >= 2:
                        sync.wait_ge(sPA.h, pa_base + (db - 1) * PAPER)
                        sync.wait_ge(sPB.h, pb_base + (db - 1) * 16)
                    sync.dma_start(out=dt_t[:, par2 * TC:(par2 + 1) * TC],
                                   in_=sp_dt[:, db * T + ch * TC: db * T + ch * TC + TC]
                                   ).then_inc(sLD.h, 16)
                    sync.dma_start(out=dtu_t[:, par2 * TC:(par2 + 1) * TC],
                                   in_=sp_dtu[:, db * T + ch * TC: db * T + ch * TC + TC]
                                   ).then_inc(sLD.h, 16)
                    if pass2:
                        sync.dma_start(out=zl_t[:, par2 * TC:(par2 + 1) * TC],
                                       in_=sp_z[:, db * T + ch * TC: db * T + ch * TC + TC]
                                       ).then_inc(sLD.h, 16)
            sLD.v += DB_I * nload * 16

            @block.scalar
            def _(s):
                for db in range(DB_I):
                    par2 = db % 2
                    s.wait_ge(sLD.h, ld_base + (db + 1) * nload * 16)
                    for n in range(NST):
                        it = db * NST + n
                        parI = it % 2
                        if it >= 2:
                            s.wait_ge(sPB.h, pb_base + it - 1)
                        s.activation(out=dA_t[:, parI * TC:(parI + 1) * TC],
                                     in_=dt_t[:, par2 * TC:(par2 + 1) * TC],
                                     func=AF.Exp,
                                     scale=a_c[:, db * NST + n: db * NST + n + 1]
                                     ).then_inc(sPA.h, 1)
                    if pass2:
                        if db >= 1:
                            s.wait_ge(sPD.h, pd_base + db)
                        s.activation(out=zs_t[:], in_=zl_t[:, par2 * TC:(par2 + 1) * TC],
                                     func=AF.Silu).then_inc(sPA.h, 1)
            sPA.v += DB_I * PAPER

            @block.vector
            def _(v):
                for db in range(DB_I):
                    par2 = db % 2
                    for n in range(NST):
                        it = db * NST + n
                        parI = it % 2
                        parH = it % 4
                        col = db * NST + n
                        v.wait_ge(sPA.h, pa_base + db * PAPER + n + 1)
                        v.tensor_tensor(out=b_t[:],
                                        in0=dtu_t[:, par2 * TC:(par2 + 1) * TC],
                                        in1=reps[:, n * 2048:n * 2048 + TC], op=OP.mult)
                        if ch == 0:
                            init = hinit[:, col:col + 1] if pass2 else 0.0
                        else:
                            hsrc = hst2 if pass2 else hst1
                            init = hsrc[:, col:col + 1]
                        if not pass2:
                            sdst = hst1 if ch == 0 else hp
                            v.tensor_tensor_scan(out=h1f[:, 0:TC - 1],
                                                 data0=dA_t[:, parI * TC:
                                                            parI * TC + TC - 1],
                                                 data1=b_t[:, 0:TC - 1],
                                                 initial=init,
                                                 op0=OP.mult, op1=OP.add)
                            v.drain()
                            v.tensor_tensor_scan(out=sdst[:, col:col + 1],
                                                 data0=dA_t[:, parI * TC + TC - 1:
                                                            parI * TC + TC],
                                                 data1=b_t[:, TC - 1:TC],
                                                 initial=h1f[:, TC - 2:TC - 1],
                                                 op0=OP.mult, op1=OP.add)
                            v.sem_inc(sPB.h, 1)
                        else:
                            v.tensor_tensor_scan(out=h_t[:, parH * TC:(parH + 1) * TC],
                                                 data0=dA_t[:, parI * TC:(parI + 1) * TC],
                                                 data1=b_t[:], initial=init,
                                                 op0=OP.mult, op1=OP.add)
                        if pass2:
                            if it >= 2:
                                v.wait_ge(sPC.h, pc_base + (it - 1) * 2)
                            v.tensor_tensor(out=yw_t[:, parI * TC:(parI + 1) * TC],
                                            in0=h_t[:, parH * TC:(parH + 1) * TC],
                                            in1=reps[:, n * 2048 + TC: n * 2048 + 2 * TC],
                                            op=OP.mult).then_inc(sPB.h, 1)
                    if pass2:
                        v.wait_ge(sPC.h, pc_base + (db + 1) * 32)
                        v.wait_ge(sPA.h, pa_base + (db + 1) * PAPER)
                        if db >= 2:
                            v.wait_ge(sSP.h, sp_base + (db - 1) * 16)
                        par2g = db % 2
                        v.tensor_tensor(out=g_t[:, par2g * TC: par2g * TC + 512],
                                        in0=PS_A[:, 0:512], in1=zs_t[:, 0:512],
                                        op=OP.mult)
                        v.tensor_tensor(out=g_t[:, par2g * TC + 512:(par2g + 1) * TC],
                                        in0=PS_B[:, 0:512], in1=zs_t[:, 512:1024],
                                        op=OP.mult).then_inc(sPD.h, 1)
            sPB.v += DB_I * NST

            if pass2:
                sPD.v += DB_I

            if pass2:
                @block.tensor
                def _(t):
                    for db in range(DB_I):
                        for n in range(NST):
                            it = db * NST + n
                            parI = it % 2
                            t.wait_ge(sPB.h, pb_base + it + 1)
                            if n == 0 and db >= 1:
                                t.wait_ge(sPD.h, pd_base + db)
                            for c2 in range(2):
                                pt = PS_A if c2 == 0 else PS_B
                                t.matmul(pt[:, 0:512],
                                         ident_b[:],
                                         yw_t[:, parI * TC + c2 * 512:
                                              parI * TC + (c2 + 1) * 512],
                                         start=(n == 0), stop=(n == NST - 1)
                                         ).then_inc(sPC.h, 1)
                sPC.v += DB_I * NST * 2

                @block.gpsimd
                def _(g):
                    for db in range(DB_I):
                        par2g = db % 2
                        g.wait_ge(sPD.h, pd_base + db + 1)
                        g.dma_start(out=sp_g[:, db * T + ch * TC: db * T + ch * TC + TC],
                                    in_=g_t[:, par2g * TC:(par2g + 1) * TC]
                                    ).then_inc(sSP.h, 16)
                sSP.v += DB_I * 16

            barrier()

        for ch in range(NCH):
            emit_reps(ch, with_c=False)
            emit_scan_pass(ch, pass2=False)

        @block.gpsimd
        def _(g):
            g.wait_ge(sBAR.h, sBAR.v)
            g.dma_start(out=cc_in[:], in_=hp[:]).then_inc(sSP.h, 16)
            g.wait_ge(sSP.h, sSP.v + 16)
            g.collective_compute("AllGather", OP.bypass,
                                 replica_groups=[list(range(8))],
                                 ins=[cc_in[:]], outs=[cc_out[:]]).then_inc(sPE.h, 1)
        sSP.v += 16
        sPE.v += 1

        @block.sync
        def _(sync):
            sync.wait_ge(sPE.h, sPE.v)
            for r in range(8):
                sync.dma_start(out=gat_t[:, r * 512:(r + 1) * 512],
                               in_=cc_out[r]).then_inc(sLD.h, 16)
        sLD.v += 8 * 16
        gat_done = sLD.v

        @block.vector
        def _(v):
            v.wait_ge(sLD.h, gat_done)
            v.memset(hcA[:], 0.0)
            for i in range(8):
                v.tensor_scalar(out=nrm_t[:, 0:256],
                                in0=gat_t[:, i * 512 + 256: i * 512 + 512],
                                scalar1=pmask[:, i:i + 1], scalar2=pmaski[:, i:i + 1],
                                op0=OP.mult, op1=OP.add)
                v.drain()
                v.tensor_tensor(out=hcB[:], in0=hcA[:], in1=nrm_t[:, 0:256], op=OP.mult)
                v.tensor_scalar(out=nrm_t[:, 0:256],
                                in0=gat_t[:, i * 512: i * 512 + 256],
                                scalar1=pmask[:, i:i + 1], scalar2=None, op0=OP.mult)
                v.drain()
                v.tensor_tensor(out=hcA[:], in0=hcB[:], in1=nrm_t[:, 0:256], op=OP.add)
            v.tensor_copy(out=hinit[:], in_=hcA[:])
            for db in range(DB_I):
                v.tensor_scalar(out=e1_t[:, 512 + db * NST:512 + (db + 1) * NST],
                                in0=a_c[:, db * NST:(db + 1) * NST],
                                scalar1=sumdt0[:, db:db + 1], scalar2=None,
                                op0=OP.mult)
            v.drain()
            v.sem_inc(sDVE.h, 1)
        sDVE.v += 1
        hin_done = sDVE.v

        @block.scalar
        def _(s):
            s.wait_ge(sDVE.h, hin_done)
            s.activation(out=ep_t[:, 0:256], in_=e1_t[:, 512:768], func=AF.Exp)
            s.drain()
            s.sem_inc(sACT.h, 1)
        sACT.v += 1
        pch0_done = sACT.v

        @block.vector
        def _(v):
            v.wait_ge(sACT.h, pch0_done)
            v.tensor_tensor(out=hcB[:], in0=ep_t[:, 0:256], in1=hinit[:], op=OP.mult)
            v.drain()
            v.tensor_tensor(out=hst2[:], in0=hst1[:], in1=hcB[:], op=OP.add)
            v.drain()
            v.sem_inc(sDVE.h, 1)
        sDVE.v += 1

        barrier()
        if upto <= 8:
            return _dump(nc, block, dbg, dbg_out, locals())

        for ch in range(NCH):
            emit_reps(ch, with_c=True)
            emit_scan_pass(ch, pass2=True)

        if upto <= 9:
            return _dump(nc, block, dbg, dbg_out, locals())

        @block.sync
        def _(sync):
            sync.wait_ge(sBAR.h, sBAR.v)
            sync.dma_start(out=x_res[:], in_=sp_xres[:]).then_inc(sLD.h, 16)
        sLD.v += 16
        xres_back = sLD.v

        w9_base = sLD.v
        cast9_base = sACT.v

        @block.sync
        def _(sync):
            for kb in range(DB_I):
                par = kb % 2
                if kb >= 2:
                    sync.wait_ge(sACT.h, cast9_base + kb - 1)
                sync.dma_start(out=sl9[par][:],
                               in_=woutT_h[kb * 128:(kb + 1) * 128, :]).then_inc(sLD.h, 16)
        sLD.v += DB_I * 16

        @block.scalar
        def _(s):
            for kb in range(DB_I):
                par = kb % 2
                s.wait_ge(sLD.h, w9_base + (kb + 1) * 16)
                s.copy(out=wob[:, kb * 1024:(kb + 1) * 1024],
                       in_=sl9[par][:]).then_inc(sACT.h, 1)
        sACT.v += DB_I
        cast9_done = sACT.v

        g9_base = sLD.v
        te9_base = sTE.v
        dve9_base = sDVE.v

        @block.sync
        def _(sync):
            for ci in range(4):
                for kb in range(DB_I):
                    j = ci * DB_I + kb
                    par = j % 2
                    if j >= 2:
                        sync.wait_ge(sTE.h, te9_base + j - 1)
                    c0 = CC_T[ci][0]
                    sync.dma_start(out=gst9[par][:],
                                   in_=sp_g[:, kb * T + c0: kb * T + c0 + 512]
                                   ).then_inc(sLD.h, 16)
        sLD.v += 4 * DB_I * 16

        @block.tensor
        def _(t):
            t.wait_ge(sACT.h, cast9_done)
            for ci in range(4):
                if ci >= 1:
                    t.wait_ge(sDVE.h, dve9_base + ci)
                for kb in range(DB_I):
                    j = ci * DB_I + kb
                    par = j % 2
                    t.wait_ge(sLD.h, g9_base + (j + 1) * 16)
                    for xo in range(DB_D):
                        ins = t.matmul(BANKS8[xo],
                                       wob[:, kb * 1024 + xo * 128: kb * 1024 + (xo + 1) * 128],
                                       gst9[par][:],
                                       start=(kb == 0), stop=(kb == DB_I - 1))
                    ins.then_inc(sTE.h, 1)
        sTE.v += 4 * DB_I

        @block.vector
        def _(v):
            v.wait_ge(sLD.h, xres_back)
            for ci in range(4):
                c0 = CC_T[ci][0]
                v.wait_ge(sTE.h, te9_base + (ci + 1) * DB_I)
                for xo in range(DB_D):
                    dcol = xo * TH + HALO + c0
                    ins = v.tensor_tensor(out=x_res[:, dcol:dcol + 512],
                                          in0=x_res[:, dcol:dcol + 512],
                                          in1=BANKS8[xo], op=OP.add)
                ins.then_inc(sDVE.h, 1)
        sDVE.v += 4

        barrier()
        if upto <= 10:
            return _dump(nc, block, dbg, dbg_out, locals())

        emit_ln(T, HALO)
        emit_ffn(T, HALO)

        te11_base = sTE.v
        act11_base = sACT.v
        sp11_base = sSP.v

        @block.tensor
        def _(t):
            for tt in range(16):
                par = tt % 2
                pt = PS_A if par == 0 else PS_B
                if tt >= 2:
                    t.wait_ge(sACT.h, act11_base + tt - 1)
                for db in range(DB_D):
                    ins = t.transpose(pt[:, 0:1024][:, db * 128:(db + 1) * 128],
                                      x_res[:, db * TH + HALO + tt * 128:
                                            db * TH + HALO + (tt + 1) * 128],
                                      ident_f[:])
                ins.then_inc(sTE.h, 1)
        sTE.v += 16

        @block.scalar
        def _(s):
            for tt in range(16):
                par = tt % 2
                pt = PS_A if par == 0 else PS_B
                s.wait_ge(sTE.h, te11_base + tt + 1)
                if tt >= 2:
                    s.wait_ge(sSP.h, sp11_base + (tt - 1) * 16)
                s.copy(out=ost_t[:, par * 1024:(par + 1) * 1024],
                       in_=pt[:, 0:1024]).then_inc(sACT.h, 1)
        sACT.v += 16

        @block.gpsimd
        def _(g):
            for tt in range(16):
                par = tt % 2
                g.wait_ge(sACT.h, act11_base + tt + 1)
                g.dma_start(out=out_h[tt * 128:(tt + 1) * 128, :],
                            in_=ost_t[:, par * 1024:(par + 1) * 1024]).then_inc(sSP.h, 16)
            g.wait_ge(sSP.h, sSP.v + 16 * 16)
        sSP.v += 16 * 16

        barrier()
        return _dump(nc, block, dbg, dbg_out, locals())


def _dump(nc, block, dbg, dbg_out, loc):
    sSP = loc["sSP"]
    if dbg:
        srcs = {
            "d_xres": loc["x_res"],
            "d_xln": loc["xln"],
            "d_xin": loc["sp_xin"],
            "d_u": loc["u_v"],
            "d_z": loc["sp_z"],
            "d_dt": loc["sp_dt"],
            "d_dtu": loc["sp_dtu"],
            "d_bm": loc["bmT_b"][:NST, :],
            "d_cm": loc["cmT_b"][:NST, :],
            "d_hp": loc["hp"][:],
            "d_hinit": loc["hinit"][:],
            "d_g": loc["sp_g"],
            "d_wq": loc["wq_b"][:],
            "d_rows": loc["AR"][0:1, 104448:104448 + 4 * 8704].bitcast(F32),
        }

        @block.gpsimd
        def _(g):
            g.wait_ge(loc["sBAR"].h, loc["sBAR"].v)
            for name in sorted(dbg):
                g.dma_start(out=dbg_out[name], in_=srcs[name]).then_inc(sSP.h, 16)
                sSP.v += 16
            g.wait_ge(sSP.h, sSP.v)
    return nc


def _host_prep(inputs):
    x = np.asarray(inputs["L0_attn_x"])
    x2 = np.ascontiguousarray(x[0])
    sel = np.zeros((NST, NST * 128), np.float32)
    for n in range(NST):
        sel[n, n * 128:(n + 1) * 128] = 1.0
    common = {
        "Wq": np.ascontiguousarray(inputs["L0_ffn_kernel"], np.float32),
        "fbias": np.ascontiguousarray(inputs["L0_ffn_bias"], np.float32),
        "la": np.ascontiguousarray(inputs["L0_ffn_a"], np.float32),
        "lb": np.ascontiguousarray(inputs["L0_ffn_b"], np.float32),
        "WinT": np.ascontiguousarray(np.asarray(inputs["L0_attn_W_in"]).T, np.float32),
        "WdtT": np.ascontiguousarray(np.asarray(inputs["L0_attn_W_dt"]).T, np.float32),
        "WBT": np.ascontiguousarray(np.asarray(inputs["L0_attn_W_B"]).T, np.float32),
        "WCT": np.ascontiguousarray(np.asarray(inputs["L0_attn_W_C"]).T, np.float32),
        "WoutT": np.ascontiguousarray(np.asarray(inputs["L0_attn_W_out"]).T, np.float32),
        "Wconv": np.ascontiguousarray(
            np.asarray(inputs["L0_attn_W_conv"]).reshape(DI, 4), np.float32),
        "bconv": np.ascontiguousarray(inputs["L0_attn_b_conv"], np.float32),
        "bdt": np.ascontiguousarray(inputs["L0_attn_b_dt"], np.float32),
        "Alog": np.ascontiguousarray(inputs["L0_attn_A_log"], np.float32),
        "ident": np.eye(128, dtype=np.float32),
        "onesm": np.ones((128, 128), np.float32),
        "sel16": sel,
    }
    in_maps = []
    for c in range(NCORES):
        lo = c * T - HALO
        if lo < 0:
            xs = np.concatenate([np.zeros((HALO, D), np.float32),
                                 x2[0:c * T + T]], axis=0)
        else:
            xs = x2[lo:c * T + T]
        pm = np.zeros((128, 8), np.float32)
        pm[:, :c] = 1.0
        m = dict(common)
        m["x"] = np.ascontiguousarray(xs, np.float32)
        m["pmask"] = pm
        m["pmaski"] = np.ascontiguousarray(1.0 - pm)
        in_maps.append(m)
    return in_maps


def kernel(**inputs):
    in_maps = _host_prep(inputs)
    nc = build_nc()
    res = run_bass_kernel_spmd(nc, in_maps, core_ids=list(range(NCORES)))
    parts = [np.asarray(res.results[c]["out"]) for c in range(NCORES)]
    full = np.concatenate(parts, axis=0)
    return full.reshape(1, NCORES * T, D).astype(np.float32)
